# version 03 (final)
# baseline (speedup 1.0000x reference)
import contextlib
import numpy as np
import ml_dtypes
import concourse.bass as bass
import concourse.mybir as mybir
from concourse.bass_utils import run_bass_kernel_spmd

F32 = mybir.dt.float32
BF16 = mybir.dt.bfloat16
AF = mybir.ActivationFunctionType
ALU = mybir.AluOpType
AX = mybir.AxisListType
BF = ml_dtypes.bfloat16

B, T, D = 2, 8192, 1024
NCORE = 8
Q = 2048
NCH = 8
QS = 512
NST = Q // QS
NH, DH = 16, 64
E, HID, DIN, DST = 8, 4096, 2048, 64
EPS_LN = 1e-5
EPS_RMS = float(np.finfo(np.float32).eps)

CONS_SEMS = ("ve", "se", "stq", "ge")


class Sched:
    ENGINES = ("sync", "tensor", "vector", "scalar", "gpsimd")

    def __init__(self, nc):
        self.nc = nc
        self.items = {e: [] for e in self.ENGINES}
        self.posted = {}
        self.handles = {}

    def make_sems(self, stack, names):
        for n in names:
            h = stack.enter_context(self.nc.semaphore(n))
            self.handles[n] = h
            self.posted[n] = 0

    def emit(self, eng, fn, inc=None, inc_by=1):
        if inc is not None:
            h = self.handles[inc]
            self.posted[inc] += inc_by

            def wrapped(e, fn=fn, h=h, n=inc_by):
                fn(e).then_inc(h, n)
            self.items[eng].append(wrapped)
        else:
            self.items[eng].append(fn)

    def dma(self, eng, out, in_, inc="ld", **kw):
        self.emit(eng, lambda e, o=out, i=in_, kw=kw: e.dma_start(out=o, in_=i, **kw),
                  inc=inc, inc_by=16)

    def wait(self, eng, name, val=None):
        v = self.posted[name] if val is None else val
        assert v <= self.posted[name], (name, v, self.posted[name])
        if v <= 0:
            return
        h = self.handles[name]
        self.items[eng].append(lambda e, h=h, v=v: e.wait_ge(h, v))

    def snap(self):
        return dict(self.posted)

    def wait_snap(self, eng, snap, sems=CONS_SEMS):
        for nm in sems:
            if nm in self.posted:
                self.wait(eng, nm, snap.get(nm, 0))

    def barrier(self, engines=None, on=None):
        engines = engines or self.ENGINES
        on = on or list(self.handles)
        for eng in engines:
            for nm in on:
                self.wait(eng, nm)

    def build(self, block):
        for engname in self.ENGINES:
            deco = getattr(block, engname)
            items = self.items[engname]

            def body(eng, items=items):
                for f in items:
                    f(eng)
            deco(body)


def emit_ln(s, x_src, nchunks, q, ps, xc, xq, xa, stat, onec, oner, out_sb,
            out_col_stride=None):
    stat_r, stat_m = stat
    """Plain LayerNorm of feature-major x into out_sb (128, nchunks*q) bf16.
    x_src(ci) -> dram AP (128, q). ps: 8 psum (128, QS). xc, xq: 2x (128, q)
    f32 scratch pairs; xa: 2x (128, QS) f32. stat: (2, q) f32.
    out_col_stride: column stride per chunk in out_sb (default q)."""
    ocs = q if out_col_stride is None else out_col_stride
    nfeat = float(nchunks * 128)
    strips = [(i * QS, QS) for i in range(q // QS)] if q >= QS else [(0, q)]
    nstr = len(strips)
    for ci in range(nchunks):
        bi = ci % 2
        if ci >= 2:
            s.wait("sync", "te")
        s.dma("sync", xc[bi][:, 0:q], x_src(ci))
        s.wait("scalar", "ld")
        if ci >= 2:
            s.wait("scalar", "te")
        s.emit("scalar", lambda e, bi=bi: e.activation(xq[bi][:, 0:q], xc[bi][:, 0:q],
                                                       AF.Square), inc="se")
        s.wait("tensor", "se")
        for si, (q0, qn) in enumerate(strips):
            s.emit("tensor", lambda e, bi=bi, si=si, q0=q0, qn=qn, ci=ci:
                   e.matmul(ps[si][0:1, 0:qn], onec[:, 0:1], xc[bi][:, q0:q0 + qn],
                            start=(ci == 0), stop=(ci == nchunks - 1)), inc="te")
            s.emit("tensor", lambda e, bi=bi, si=si, q0=q0, qn=qn, ci=ci:
                   e.matmul(ps[4 + si][0:1, 0:qn], onec[:, 0:1], xq[bi][:, q0:q0 + qn],
                            start=(ci == 0), stop=(ci == nchunks - 1)), inc="te")
    mu = xc[0]
    ms = xq[0]
    s.wait("vector", "te")
    for si, (q0, qn) in enumerate(strips):
        s.emit("vector", lambda e, si=si, q0=q0, qn=qn:
               e.tensor_scalar_mul(mu[0:1, q0:q0 + qn], ps[si][0:1, 0:qn], 1.0 / nfeat))
        s.emit("vector", lambda e, si=si, q0=q0, qn=qn:
               e.tensor_scalar_mul(ms[0:1, q0:q0 + qn], ps[4 + si][0:1, 0:qn], 1.0 / nfeat),
               inc="ve")
    s.emit("vector", lambda e: e.tensor_mul(stat_m[0:1, 0:q], mu[0:1, 0:q], mu[0:1, 0:q]))
    s.emit("vector", lambda e: e.tensor_sub(ms[0:1, 0:q], ms[0:1, 0:q], stat_m[0:1, 0:q]),
           inc="ve")
    s.emit("vector", lambda e: e.tensor_scalar_add(ms[0:1, 0:q], ms[0:1, 0:q], EPS_LN),
           inc="ve")
    s.wait("scalar", "ve")
    s.emit("scalar", lambda e: e.activation(stat_r[0:1, 0:q], ms[0:1, 0:q], AF.Sqrt),
           inc="se")
    s.wait("vector", "se")
    s.emit("vector", lambda e: e.reciprocal(stat_r[0:1, 0:q], stat_r[0:1, 0:q]), inc="ve")
    s.emit("vector", lambda e: e.tensor_mul(stat_m[0:1, 0:q], mu[0:1, 0:q], stat_r[0:1, 0:q]),
           inc="ve")
    for si, (q0, qn) in enumerate(strips):
        s.wait("tensor", "ve")
        s.emit("tensor", lambda e, q0=q0, qn=qn:
               e.matmul(ps[0][:, 0:qn], oner[0:1, :], stat_r[0:1, q0:q0 + qn],
                        start=True, stop=True), inc="te")
        s.emit("tensor", lambda e, q0=q0, qn=qn:
               e.matmul(ps[1][:, 0:qn], oner[0:1, :], stat_m[0:1, q0:q0 + qn],
                        start=True, stop=True), inc="te")
        for ci in range(nchunks):
            bi = ci % 2
            if si > 0 or ci >= 2:
                s.wait("sync", "ve")
            s.dma("sync", xa[bi][:, 0:qn], x_src(ci)[:, q0:q0 + qn])
            s.wait("vector", "ld")
            s.wait("vector", "te")
            s.emit("vector", lambda e, bi=bi, qn=qn:
                   e.tensor_mul(xa[bi][:, 0:qn], xa[bi][:, 0:qn], ps[0][:, 0:qn]))
            s.emit("vector", lambda e, bi=bi, ci=ci, q0=q0, qn=qn:
                   e.tensor_sub(out_sb[:, ci * ocs + q0:ci * ocs + q0 + qn],
                                xa[bi][:, 0:qn], ps[1][:, 0:qn]), inc="ve")


def emit_proj(s, ps, w_src, wbuf, rhs_sb, out_fn, nchunks_in, njc, q=Q,
              rhs_col_stride=None, banks=None):
    rcs = q if rhs_col_stride is None else rhs_col_stride
    wcols = njc * 128
    banks = banks if banks is not None else list(range(8))
    strips = [(i * QS, QS) for i in range(q // QS)] if q >= QS else [(0, q)]
    s.barrier(engines=("sync",))
    for ci in range(nchunks_in):
        s.dma("sync", wbuf[:, ci * wcols:(ci + 1) * wcols], w_src(ci), inc="ldw")
    s.wait("tensor", "ldw")
    bank_snap = {}
    k = 0
    for jc in range(njc):
        for si, (q0, qn) in enumerate(strips):
            bk = banks[k % len(banks)]
            k += 1
            pj = ps[bk]
            if bk in bank_snap:
                s.wait_snap("tensor", bank_snap[bk])
            for ci in range(nchunks_in):
                s.emit("tensor", lambda e, pj=pj, jc=jc, ci=ci, q0=q0, qn=qn:
                       e.matmul(pj[:, 0:qn],
                                wbuf[:, ci * wcols + jc * 128:ci * wcols + (jc + 1) * 128],
                                rhs_sb[:, ci * rcs + q0:ci * rcs + q0 + qn],
                                start=(ci == 0), stop=(ci == nchunks_in - 1)),
                       inc="te")
            out_fn(jc, si, pj, q0, qn)
            bank_snap[bk] = s.snap()


def build_cast(n_tiles, width):
    nc = bass.Bass("TRN2", target_bir_lowering=False)
    xin = nc.dram_tensor("x", [n_tiles, 128, width], F32, kind="ExternalInput").ap()
    yout = nc.dram_tensor("y", [n_tiles, 128, width], BF16, kind="ExternalOutput").ap()
    with contextlib.ExitStack() as st:
        s = Sched(nc)
        s.make_sems(st, ["ld", "ve", "stq"])
        bufs = [st.enter_context(nc.sbuf_tensor(f"b{i}", [128, width], F32)) for i in range(2)]
        obufs = [st.enter_context(nc.sbuf_tensor(f"o{i}", [128, width], BF16)) for i in range(2)]
        block = st.enter_context(nc.Block())
        for i in range(n_tiles):
            bi = i % 2
            if i >= 2:
                s.wait("sync", "ve", i - 1)
            s.dma("sync", bufs[bi][:], xin[i])
            s.wait("vector", "ld", 16 * (i + 1))
            if i >= 2:
                s.wait("vector", "stq", 16 * (i - 1))
            s.emit("vector", lambda e, bi=bi: e.tensor_copy(obufs[bi][:], bufs[bi][:]),
                   inc="ve")
            s.wait("gpsimd", "ve", i + 1)
            s.dma("gpsimd", yout[i], obufs[bi][:], inc="stq")
        s.wait("gpsimd", "stq", 16 * n_tiles)
        s.build(block)
    return nc


_prog_cache = {}


def _get(name, builder, *args):
    if name not in _prog_cache:
        _prog_cache[name] = builder(*args)
    return _prog_cache[name]


def cast_weights_bf16(arrays):
    flat = np.concatenate([np.ascontiguousarray(a).ravel() for a in arrays])
    width = 2048
    per = 128 * width
    n_total = -(-flat.size // (per * NCORE)) * NCORE
    pad = n_total * per - flat.size
    flat = np.concatenate([flat, np.zeros(pad, np.float32)])
    n_tiles = n_total // NCORE
    nc = _get(f"cast{n_tiles}", build_cast, n_tiles, width)
    shards = flat.reshape(NCORE, n_tiles, 128, width)
    res = run_bass_kernel_spmd(nc, [{"x": shards[i]} for i in range(NCORE)],
                               core_ids=list(range(NCORE)))
    out = np.concatenate([r["y"].ravel() for r in res.results])
    if pad:
        out = out[:-pad]
    outs, ofs = [], 0
    for a in arrays:
        outs.append(np.ascontiguousarray(out[ofs:ofs + a.size].reshape(a.shape)))
        ofs += a.size
    return outs


def build_rwkv():
    nc = bass.Bass("TRN2", target_bir_lowering=False)
    x_in = nc.dram_tensor("x", [D, Q], F32, kind="ExternalInput").ap()
    xp_in = nc.dram_tensor("xp", [D, Q], F32, kind="ExternalInput").ap()
    wr_in = nc.dram_tensor("wr", [D, D], BF16, kind="ExternalInput").ap()
    wk_in = nc.dram_tensor("wk", [D, D], BF16, kind="ExternalInput").ap()
    wv_in = nc.dram_tensor("wv", [D, D], BF16, kind="ExternalInput").ap()
    mur_in = nc.dram_tensor("mur", [128, NCH], F32, kind="ExternalInput").ap()
    muk_in = nc.dram_tensor("muk", [128, NCH], F32, kind="ExternalInput").ap()
    onec_in = nc.dram_tensor("onec", [128, 1], F32, kind="ExternalInput").ap()
    oner_in = nc.dram_tensor("oner", [1, 128], F32, kind="ExternalInput").ap()
    out_t = nc.dram_tensor("out", [D, Q], F32, kind="ExternalOutput").ap()

    with contextlib.ExitStack() as st:
        s = Sched(nc)
        s.make_sems(st, ["ld", "ldw", "te", "ve", "se", "ge", "stq"])
        sb = lambda name, shp, dt=F32: st.enter_context(nc.sbuf_tensor(name, shp, dt))
        ps = [st.enter_context(nc.psum_tensor(f"ps{i}", [128, QS], F32)) for i in range(8)]
        xc = [sb("xc0", [128, Q]), sb("xc1", [128, Q])]
        xq = [sb("xq0", [128, Q]), sb("xq1", [128, Q])]
        xa = [sb("xa0", [128, QS]), sb("xa1", [128, QS])]
        stat = (xq[1], xc[1])
        onec = sb("s_onec", [128, 1])
        oner = sb("s_oner", [1, 128])
        mur = sb("s_mur", [128, NCH])
        muk = sb("s_muk", [128, NCH])
        bufA = sb("bufA", [128, NCH * Q], BF16)
        bufB = sb("bufB", [128, NCH * Q], BF16)
        bufC = sb("bufC", [128, NCH * Q], BF16)
        bufD = sb("bufD", [128, NCH * Q], BF16)
        dsc = sb("dsc", [128, Q])
        wbuf = sb("wbuf", [128, NCH * 1024], BF16)
        block = st.enter_context(nc.Block())

        s.dma("sync", onec[:], onec_in)
        s.dma("sync", oner[:], oner_in)
        s.dma("sync", mur[:], mur_in)
        s.dma("sync", muk[:], muk_in)

        emit_ln(s, lambda ci: x_in[ci * 128:(ci + 1) * 128, :], NCH, Q,
                ps, xc, xq, xa, stat, onec, oner, bufA)
        s.barrier()
        emit_ln(s, lambda ci: xp_in[ci * 128:(ci + 1) * 128, :], NCH, Q,
                ps, xc, xq, xa, stat, onec, oner, bufB)
        s.barrier()
        for ci in range(NCH):
            c0 = ci * Q
            s.emit("vector", lambda e, c0=c0: e.tensor_sub(
                dsc[:, :], bufA[:, c0:c0 + Q], bufB[:, c0:c0 + Q]), inc="ve")
            s.emit("vector", lambda e, c0=c0, ci=ci: e.scalar_tensor_tensor(
                bufC[:, c0:c0 + Q], dsc[:, :], mur[:, ci:ci + 1],
                bufB[:, c0:c0 + Q], ALU.mult, ALU.add), inc="ve")
            s.emit("vector", lambda e, c0=c0, ci=ci: e.scalar_tensor_tensor(
                bufD[:, c0:c0 + Q], dsc[:, :], muk[:, ci:ci + 1],
                bufB[:, c0:c0 + Q], ALU.mult, ALU.add), inc="ve")

        s.barrier()

        def out_r(jc, si, pj, q0, qn):
            s.wait("scalar", "te")
            s.emit("scalar", lambda e, jc=jc, q0=q0, qn=qn, pj=pj: e.activation(
                bufA[:, jc * Q + q0:jc * Q + q0 + qn], pj[:, 0:qn], AF.Sigmoid),
                inc="se")
        emit_proj(s, ps, lambda ci: wr_in[ci * 128:(ci + 1) * 128, :], wbuf,
                  bufC, out_r, NCH, NCH)

        s.barrier()

        def out_k(jc, si, pj, q0, qn):
            s.wait("vector", "te")
            s.wait("vector", "se")
            s.emit("vector", lambda e, pj=pj, qn=qn:
                   e.tensor_scalar_max(xa[0][:, 0:qn], pj[:, 0:qn], 0.0), inc="ve")
            s.wait("scalar", "ve")
            s.emit("scalar", lambda e, jc=jc, q0=q0, qn=qn: e.activation(
                bufB[:, jc * Q + q0:jc * Q + q0 + qn], xa[0][:, 0:qn], AF.Square),
                inc="se")
        emit_proj(s, ps, lambda ci: wk_in[ci * 128:(ci + 1) * 128, :], wbuf,
                  bufD, out_k, NCH, NCH)

        s.barrier()

        def out_v(jc, si, pj, q0, qn):
            s.wait("sync", "ve")
            s.dma("sync", xa[1][:, 0:qn], x_in[jc * 128:(jc + 1) * 128, q0:q0 + qn])
            s.wait("vector", "te")
            s.wait("vector", "ld")
            s.wait("vector", "stq")
            s.emit("vector", lambda e, pj=pj, jc=jc, q0=q0, qn=qn: e.tensor_mul(
                xa[0][:, 0:qn], pj[:, 0:qn], bufA[:, jc * Q + q0:jc * Q + q0 + qn]))
            s.emit("vector", lambda e, q0=q0, qn=qn: e.scalar_tensor_tensor(
                dsc[:, 0:qn], xa[0][:, 0:qn], 0.5, xa[1][:, 0:qn], ALU.mult, ALU.add),
                inc="ve")
            s.wait("gpsimd", "ve")
            s.dma("gpsimd", out_t[jc * 128:(jc + 1) * 128, q0:q0 + qn], dsc[:, 0:qn],
                  inc="stq")
        emit_proj(s, ps, lambda ci: wv_in[ci * 128:(ci + 1) * 128, :], wbuf,
                  bufB, out_v, NCH, NCH)
        s.wait("gpsimd", "stq")
        s.build(block)
    return nc


def run_rwkv(xT, wrT_bf, wkT_bf, wvT_bf, mu_r, mu_k):
    nc = _get("rwkv", build_rwkv)
    onec = np.ones((128, 1), np.float32)
    oner = np.ones((1, 128), np.float32)
    murm = np.ascontiguousarray(mu_r.reshape(NCH, 128).T)
    mukm = np.ascontiguousarray(mu_k.reshape(NCH, 128).T)
    in_maps = []
    for c in range(NCORE):
        b, t0 = c // 4, (c % 4) * Q
        xp = np.zeros((D, Q), np.float32)
        if t0 == 0:
            xp[:, 1:] = xT[b][:, 0:Q - 1]
        else:
            xp[:, :] = xT[b][:, t0 - 1:t0 + Q - 1]
        in_maps.append(dict(x=np.ascontiguousarray(xT[b][:, t0:t0 + Q]), xp=xp,
                            wr=wrT_bf, wk=wkT_bf, wv=wvT_bf,
                            mur=murm, muk=mukm, onec=onec, oner=oner))
    res = run_bass_kernel_spmd(nc, in_maps, core_ids=list(range(NCORE)))
    out = np.empty((B, D, T), np.float32)
    for c in range(NCORE):
        b, t0 = c // 4, (c % 4) * Q
        out[b][:, t0:t0 + Q] = res.results[c]["out"]
    return out


def build_moe():
    nc = bass.Bass("TRN2", target_bir_lowering=False)
    x_in = nc.dram_tensor("x", [D, Q], F32, kind="ExternalInput").ap()
    w1_in = nc.dram_tensor("w1", [E, D, HID], BF16, kind="ExternalInput").ap()
    w2_in = nc.dram_tensor("w2", [E, HID, D], BF16, kind="ExternalInput").ap()
    ws1_in = nc.dram_tensor("ws1", [D, HID], BF16, kind="ExternalInput").ap()
    ws2_in = nc.dram_tensor("ws2", [HID, D], BF16, kind="ExternalInput").ap()
    gw_in = nc.dram_tensor("gw", [D, E], F32, kind="ExternalInput").ap()
    gbr_in = nc.dram_tensor("gbr", [1, E], F32, kind="ExternalInput").ap()
    gb_in = nc.dram_tensor("gb", [E, 1], F32, kind="ExternalInput").ap()
    b1_in = nc.dram_tensor("b1", [E, 128, HID // 128], F32, kind="ExternalInput").ap()
    b2_in = nc.dram_tensor("b2", [E, 128, NCH], F32, kind="ExternalInput").ap()
    bs1_in = nc.dram_tensor("bs1", [128, HID // 128], F32, kind="ExternalInput").ap()
    bs2_in = nc.dram_tensor("bs2", [128, NCH], F32, kind="ExternalInput").ap()
    eye_in = nc.dram_tensor("eye", [128, 128], F32, kind="ExternalInput").ap()
    onec_in = nc.dram_tensor("onec", [128, 1], F32, kind="ExternalInput").ap()
    oner_in = nc.dram_tensor("oner", [1, 128], F32, kind="ExternalInput").ap()
    onerb_in = nc.dram_tensor("onerb", [1, 128], BF16, kind="ExternalInput").ap()
    out_t = nc.dram_tensor("out", [D, Q], F32, kind="ExternalOutput").ap()
    NHC = HID // 128

    with contextlib.ExitStack() as st:
        s = Sched(nc)
        s.make_sems(st, ["ld", "ldw", "te", "ve", "se", "ge", "stq"])
        sb = lambda name, shp, dt=F32: st.enter_context(nc.sbuf_tensor(name, shp, dt))
        ps = [st.enter_context(nc.psum_tensor(f"ps{i}", [128, QS], F32)) for i in range(8)]
        xc = [sb("xc0", [128, Q]), sb("xc1", [128, Q])]
        xq = [sb("xq0", [128, Q]), sb("xq1", [128, Q])]
        xa = [sb("xa0", [128, QS]), sb("xa1", [128, QS])]
        stat = (xq[1], xc[1])
        onec = sb("s_onec", [128, 1])
        oner = sb("s_oner", [1, 128])
        eye = sb("s_eye", [128, 128])
        gsb = sb("s_gw", [128, NCH * E])
        gbr = sb("s_gbr", [1, E])
        csr = sb("s_csr", [1, E])
        csrep = sb("s_csrep", [128, E])
        gbrep = sb("s_gbrep", [128, E])
        rstdq = sb("s_rstdq", [128, 16])
        mrsq = sb("s_mrsq", [128, 16])
        gbsb = sb("s_gb", [E, 1])
        b1sb = sb("s_b1", [128, E * NHC])
        b2sb = sb("s_b2", [128, E * NCH])
        bs1sb = sb("s_bs1", [128, NHC])
        bs2sb = sb("s_bs2", [128, NCH])
        xln = sb("xln", [128, NCH * Q], BF16)
        acc = sb("acc", [128, NCH * Q])
        hsb = sb("hsb", [128, NHC * QS], BF16)
        w1b = [sb(f"w1b{i}", [128, 1024], BF16) for i in range(2)]
        w2b = [sb(f"w2b{i}", [128, 1024], BF16) for i in range(2)]
        lgsb = sb("lgsb", [E, Q])
        lq = sb("lq", [128, 128])
        tops = sb("tops", [128, 7 * 16])
        comb = sb("comb", [128, 128])
        combT = sb("combT", [E, Q], BF16)
        crsm = sb("crsm", [1, QS], BF16)
        onerb = sb("s_onerb", [1, 128], BF16)
        crep = sb("crep", [128, QS])
        tmpv = sb("tmpv", [128, QS])
        block = st.enter_context(nc.Block())

        for dst, src_ in [(onec, onec_in), (oner, oner_in), (eye, eye_in),
                          (onerb, onerb_in), (gbr, gbr_in),
                          (gbsb, gb_in), (bs1sb, bs1_in), (bs2sb, bs2_in)]:
            s.dma("sync", dst[:], src_)
        for ci in range(NCH):
            s.dma("sync", gsb[:, ci * E:(ci + 1) * E], gw_in[ci * 128:(ci + 1) * 128, :])
        for e in range(E):
            s.dma("sync", b1sb[:, e * NHC:(e + 1) * NHC], b1_in[e])
            s.dma("sync", b2sb[:, e * NCH:(e + 1) * NCH], b2_in[e])

        emit_ln(s, lambda ci: x_in[ci * 128:(ci + 1) * 128, :], NCH, Q,
                ps, xc, xq, xa, stat, onec, oner, xln)
        s.emit("vector", lambda e: e.memset(acc[:, :], 0.0), inc="ve")
        s.barrier()

        stat_r, stat_m = stat
        for ci in range(NCH):
            s.emit("tensor", lambda e, ci=ci: e.matmul(
                ps[6][0:1, 0:E], onec[:, 0:1], gsb[:, ci * E:(ci + 1) * E],
                start=(ci == 0), stop=(ci == NCH - 1)), inc="te")
        s.wait("vector", "te")
        s.emit("vector", lambda e: e.tensor_copy(csr[0:1, :], ps[6][0:1, 0:E]), inc="ve")
        s.wait("tensor", "ve")
        s.emit("tensor", lambda e: e.matmul(ps[6][:, 0:E], oner[0:1, :], csr[0:1, :],
                                            start=True, stop=True), inc="te")
        s.emit("tensor", lambda e: e.matmul(ps[5][:, 0:E], oner[0:1, :], gbr[0:1, :],
                                            start=True, stop=True), inc="te")
        s.wait("vector", "te")
        s.emit("vector", lambda e: e.tensor_copy(csrep[:, :], ps[6][:, 0:E]))
        s.emit("vector", lambda e: e.tensor_copy(gbrep[:, :], ps[5][:, 0:E]), inc="ve")
        for si in range(NST):
            q0 = si * QS
            s.wait("tensor", "ve")
            for ci in range(NCH):
                bi = ci % 2
                if si > 0 or ci >= 2:
                    s.wait("sync", "te")
                s.dma("sync", xa[bi][:, :], x_in[ci * 128:(ci + 1) * 128, q0:q0 + QS])
                s.wait("tensor", "ld")
                s.emit("tensor", lambda e, si=si, ci=ci, bi=bi:
                       e.matmul(ps[6][0:E, :], gsb[:, ci * E:(ci + 1) * E],
                                xa[bi][:, :],
                                start=(ci == 0), stop=(ci == NCH - 1)), inc="te")
            s.wait("vector", "te")
            s.emit("vector", lambda e, q0=q0: e.tensor_copy(
                lgsb[:, q0:q0 + QS], ps[6][0:E, :]), inc="ve")
        tsnap = None
        for ti in range(16):
            s.wait("tensor", "ve")
            if tsnap is not None:
                s.wait_snap("tensor", tsnap)
            s.emit("tensor", lambda e, ti=ti: e.transpose(
                ps[7][:, 0:1], stat_r[0:1, ti * 128:(ti + 1) * 128], eye[0:1, 0:1]),
                inc="te")
            s.wait("vector", "te")
            s.emit("vector", lambda e, ti=ti: e.tensor_copy(rstdq[:, ti:ti + 1],
                                                            ps[7][:, 0:1]), inc="ve")
            tsnap = s.snap()
            s.wait_snap("tensor", tsnap)
            s.emit("tensor", lambda e, ti=ti: e.transpose(
                ps[7][:, 0:1], stat_m[0:1, ti * 128:(ti + 1) * 128], eye[0:1, 0:1]),
                inc="te")
            s.wait("vector", "te")
            s.emit("vector", lambda e, ti=ti: e.tensor_copy(mrsq[:, ti:ti + 1],
                                                            ps[7][:, 0:1]), inc="ve")
            tsnap = s.snap()
        tsnap = None
        for ti in range(16):
            s.wait("tensor", "ve")
            if tsnap is not None:
                s.wait_snap("tensor", tsnap)
            s.emit("tensor", lambda e, ti=ti: e.transpose(
                ps[7][:, 0:E], lgsb[0:E, ti * 128:(ti + 1) * 128], eye[0:E, 0:E]),
                inc="te")
            s.wait("vector", "te")
            s.emit("vector", lambda e, ti=ti: e.tensor_copy(lq[:, ti * E:(ti + 1) * E],
                                                            ps[7][:, 0:E]), inc="ve")
            tsnap = s.snap()
        lq3 = lq[:, :].rearrange("p (t e) -> p t e", e=E)
        rstdqb = rstdq[:, :].rearrange("p (t u) -> p t u", u=1).broadcast_to([128, 16, E])
        mrsqb = mrsq[:, :].rearrange("p (t u) -> p t u", u=1).broadcast_to([128, 16, E])
        csrepb = csrep[:, :].rearrange("p (u e) -> p u e", u=1).broadcast_to([128, 16, E])
        gbrepb = gbrep[:, :].rearrange("p (u e) -> p u e", u=1).broadcast_to([128, 16, E])
        s.barrier(engines=("vector",))
        s.emit("vector", lambda e: e.tensor_tensor(lq3, lq3, rstdqb, ALU.mult), inc="ve")
        cmb3 = comb[:, :].rearrange("p (t e) -> p t e", e=E)
        s.emit("vector", lambda e: e.tensor_tensor(cmb3, mrsqb, csrepb, ALU.mult), inc="ve")
        s.emit("vector", lambda e: e.tensor_tensor(lq3, lq3, cmb3, ALU.subtract), inc="ve")
        s.emit("vector", lambda e: e.tensor_tensor(lq3, lq3, gbrepb, ALU.add), inc="ve")
        m1 = tops[:, 0:16]
        m2 = tops[:, 16:32]
        s1w = tops[:, 32:48]
        s2w = tops[:, 48:64]
        dif = tops[:, 64:80]
        eq1 = sb("eq1", [128, 128])
        eq2 = sb("eq2", [128, 128])
        l2s = sb("l2s", [128, 128])
        eq13 = eq1[:, :].rearrange("p (t e) -> p t e", e=E)
        eq23 = eq2[:, :].rearrange("p (t e) -> p t e", e=E)
        l2s3 = l2s[:, :].rearrange("p (t e) -> p t e", e=E)
        comb3 = comb[:, :].rearrange("p (t e) -> p t e", e=E)
        m1b = m1.rearrange("p (t u) -> p t u", u=1).broadcast_to([128, 16, E])
        m2b = m2.rearrange("p (t u) -> p t u", u=1).broadcast_to([128, 16, E])
        s1b = s1w.rearrange("p (t u) -> p t u", u=1).broadcast_to([128, 16, E])
        s2b = s2w.rearrange("p (t u) -> p t u", u=1).broadcast_to([128, 16, E])
        s.barrier(engines=("vector",))
        s.emit("vector", lambda e: e.reduce_max(m1[:, :], lq3, axis=AX.X), inc="ve")
        s.wait("vector", "ve")
        s.emit("vector", lambda e: e.tensor_tensor(eq13, lq3, m1b, ALU.is_equal))
        s.emit("vector", lambda e: e.scalar_tensor_tensor(l2s3, eq13, -1e30, lq3,
                                                          ALU.mult, ALU.add), inc="ve")
        s.emit("vector", lambda e: e.reduce_max(m2[:, :], l2s3, axis=AX.X), inc="ve")
        s.wait("vector", "ve")
        s.emit("vector", lambda e: e.tensor_sub(dif[:, :], m2[:, :], m1[:, :]), inc="ve")
        s.wait("scalar", "ve")
        s.emit("scalar", lambda e: e.activation(s2w[:, :], dif[:, :], AF.Sigmoid), inc="se")
        s.wait("vector", "se")
        s.emit("vector", lambda e: e.tensor_scalar(s1w[:, :], s2w[:, :], -1.0, 1.0,
                                                   ALU.mult, ALU.add), inc="ve")
        s.emit("vector", lambda e: e.tensor_tensor(eq23, l2s3, m2b, ALU.is_equal), inc="ve")
        s.emit("vector", lambda e: e.tensor_tensor(comb3, eq13, s1b, ALU.mult), inc="ve")
        s.emit("vector", lambda e: e.tensor_tensor(eq13, eq23, s2b, ALU.mult), inc="ve")
        s.emit("vector", lambda e: e.tensor_add(comb[:, :], comb[:, :], eq1[:, :]), inc="ve")
        tsnap = None
        for ti in range(16):
            s.wait("tensor", "ve")
            if tsnap is not None:
                s.wait_snap("tensor", tsnap)
            s.emit("tensor", lambda e, ti=ti: e.transpose(
                ps[7][0:E, 0:128], comb[:, ti * E:(ti + 1) * E], eye[:, :]), inc="te")
            s.wait("vector", "te")
            s.emit("vector", lambda e, ti=ti: e.tensor_copy(
                combT[0:E, ti * 128:(ti + 1) * 128], ps[7][0:E, 0:128]), inc="ve")
            tsnap = s.snap()
        s.barrier()

        bank_snap = {}
        wsnap = {0: None, 1: None}
        w2snap = {0: None, 1: None}

        def h_phase(w_src_fn, bias_sb, bias_off, si):
            q0 = si * QS
            for jg in range(4):
                for ci in range(NCH):
                    bi = ci % 2
                    if wsnap[bi] is not None:
                        s.wait_snap("sync", wsnap[bi], sems=("te",))
                    s.dma("sync", w1b[bi][:], w_src_fn(ci, jg), inc="ldw")
                    s.wait("tensor", "ldw")
                    for j in range(8):
                        bk = jg * 8 + j
                        if ci == 0 and bank_snap.get(j) is not None:
                            s.wait_snap("tensor", bank_snap[j])
                        s.emit("tensor", lambda e, bi=bi, j=j, ci=ci, q0=q0:
                               e.matmul(ps[j][:, :], w1b[bi][:, j * 128:(j + 1) * 128],
                                        xln[:, ci * Q + q0:ci * Q + q0 + QS],
                                        start=(ci == 0), stop=(ci == NCH - 1)), inc="te")
                    wsnap[bi] = {"te": s.posted["te"]}
                s.wait("scalar", "te")
                for j in range(8):
                    hc = jg * 8 + j
                    s.emit("scalar", lambda e, j=j, hc=hc, bias_off=bias_off:
                           e.activation(hsb[:, hc * QS:(hc + 1) * QS], ps[j][:, :],
                                        AF.Gelu, bias=bias_sb[:, bias_off + hc:bias_off + hc + 1]),
                           inc="se")
                    bank_snap[j] = s.snap()

        def z_phase(w_src_fn, bias_sb, bias_off, si, is_shared):
            q0 = si * QS
            for hc in range(NHC):
                bi = hc % 2
                if w2snap[bi] is not None:
                    s.wait_snap("sync", w2snap[bi], sems=("te",))
                s.dma("sync", w2b[bi][:], w_src_fn(hc), inc="ldw")
                s.wait("tensor", "ldw")
                if hc == 0:
                    s.wait("tensor", "se")
                for jo in range(8):
                    if hc == 0 and bank_snap.get(jo) is not None:
                        s.wait_snap("tensor", bank_snap[jo])
                    s.emit("tensor", lambda e, bi=bi, jo=jo, hc=hc:
                           e.matmul(ps[jo][:, :], w2b[bi][:, jo * 128:(jo + 1) * 128],
                                    hsb[:, hc * QS:(hc + 1) * QS],
                                    start=(hc == 0), stop=(hc == NHC - 1)), inc="te")
                w2snap[bi] = {"te": s.posted["te"]}
            s.wait("vector", "te")
            for jo in range(8):
                if is_shared:
                    s.emit("vector", lambda e, jo=jo, bias_off=bias_off:
                           e.tensor_scalar(tmpv[:, :], ps[jo][:, :],
                                           bias_sb[:, bias_off + jo:bias_off + jo + 1],
                                           None, ALU.add))
                else:
                    s.emit("vector", lambda e, jo=jo, bias_off=bias_off:
                           e.scalar_tensor_tensor(tmpv[:, :], ps[jo][:, :],
                                                  bias_sb[:, bias_off + jo:bias_off + jo + 1],
                                                  crep[:, :], ALU.add, ALU.mult))
                s.emit("vector", lambda e, jo=jo, q0=q0: e.tensor_add(
                    acc[:, jo * Q + q0:jo * Q + q0 + QS],
                    acc[:, jo * Q + q0:jo * Q + q0 + QS], tmpv[:, :]), inc="ve")
                bank_snap[jo] = s.snap()

        for e_ in range(E):
            for si in range(NST):
                s.wait("sync", "te")
                s.dma("sync", crsm[0:1, :], combT[e_:e_ + 1, si * QS:(si + 1) * QS])
                s.wait_snap("tensor", bank_snap.get(6) or {}, sems=("ve",))
                s.wait("tensor", "ld")
                s.emit("tensor", lambda e, e_=e_, si=si: e.matmul(
                    ps[6][:, :], onerb[0:1, :], crsm[0:1, :],
                    start=True, stop=True), inc="te")
                s.wait("vector", "te")
                s.wait("vector", "ve")
                s.emit("vector", lambda e: e.tensor_copy(crep[:, :], ps[6][:, :]),
                       inc="ve")
                bank_snap[6] = s.snap()
                h_phase(lambda ci, jg, e_=e_: w1_in[e_, ci * 128:(ci + 1) * 128,
                                                   jg * 1024:(jg + 1) * 1024],
                        b1sb, e_ * NHC, si)
                z_phase(lambda hc, e_=e_: w2_in[e_, hc * 128:(hc + 1) * 128, :],
                        b2sb, e_ * NCH, si, False)
        for si in range(NST):
            h_phase(lambda ci, jg: ws1_in[ci * 128:(ci + 1) * 128,
                                          jg * 1024:(jg + 1) * 1024],
                    bs1sb, 0, si)
            z_phase(lambda hc: ws2_in[hc * 128:(hc + 1) * 128, :],
                    bs2sb, 0, si, True)

        s.barrier()
        for ci in range(NCH):
            for si in range(NST):
                q0 = si * QS
                bi = (ci * NST + si) % 2
                if ci * NST + si >= 2:
                    s.wait("sync", "ve")
                s.dma("sync", xa[bi][:, :], x_in[ci * 128:(ci + 1) * 128, q0:q0 + QS])
                s.wait("vector", "ld")
                s.wait("vector", "stq")
                s.emit("vector", lambda e, ci=ci, q0=q0, bi=bi: e.scalar_tensor_tensor(
                    tmpv[:, :], acc[:, ci * Q + q0:ci * Q + q0 + QS], 0.5, xa[bi][:, :],
                    ALU.mult, ALU.add), inc="ve")
                s.wait("gpsimd", "ve")
                s.dma("gpsimd", out_t[ci * 128:(ci + 1) * 128, q0:q0 + QS],
                      tmpv[:, :], inc="stq")
        s.wait("gpsimd", "stq")
        s.build(block)
    return nc


def run_moe(xT, w1_bf, w2_bf, ws1_bf, ws2_bf, gw_f32, gate_b, b1, b2, bs1, bs2):
    nc = _get("moe", build_moe)
    onec = np.ones((128, 1), np.float32)
    oner = np.ones((1, 128), np.float32)
    eye = np.eye(128, dtype=np.float32)
    b1t = np.ascontiguousarray(b1.reshape(E, HID // 128, 128).transpose(0, 2, 1))
    b2t = np.ascontiguousarray(b2.reshape(E, NCH, 128).transpose(0, 2, 1))
    bs1t = np.ascontiguousarray(bs1.reshape(HID // 128, 128).T)
    bs2t = np.ascontiguousarray(bs2.reshape(NCH, 128).T)
    gb = np.ascontiguousarray(gate_b.reshape(E, 1))
    in_maps = []
    for c in range(NCORE):
        b_, t0 = c // 4, (c % 4) * Q
        in_maps.append(dict(x=np.ascontiguousarray(xT[b_][:, t0:t0 + Q]),
                            w1=w1_bf, w2=w2_bf, ws1=ws1_bf, ws2=ws2_bf,
                            gw=np.ascontiguousarray(gw_f32, np.float32), gb=gb,
                            gbr=np.ascontiguousarray(gate_b.reshape(1, E), np.float32),
                            b1=b1t, b2=b2t, bs1=bs1t, bs2=bs2t,
                            eye=eye, onec=onec, oner=oner,
                            onerb=oner.astype(BF)))
    res = run_bass_kernel_spmd(nc, in_maps, core_ids=list(range(NCORE)))
    out = np.empty((B, D, T), np.float32)
    for c in range(NCORE):
        b_, t0 = c // 4, (c % 4) * Q
        out[b_][:, t0:t0 + Q] = res.results[c]["out"]
    return out


def build_attnpre():
    nc = bass.Bass("TRN2", target_bir_lowering=False)
    x_in = nc.dram_tensor("x", [D, Q], F32, kind="ExternalInput").ap()
    wq_in = nc.dram_tensor("wq", [D, D], BF16, kind="ExternalInput").ap()
    wk_in = nc.dram_tensor("wk", [D, D], BF16, kind="ExternalInput").ap()
    wv_in = nc.dram_tensor("wv", [D, D], BF16, kind="ExternalInput").ap()
    qnw_in = nc.dram_tensor("qnw", [128, 1], F32, kind="ExternalInput").ap()
    knw_in = nc.dram_tensor("knw", [128, 1], F32, kind="ExternalInput").ap()
    sel2_in = nc.dram_tensor("sel2", [2, 128], F32, kind="ExternalInput").ap()
    blk2_in = nc.dram_tensor("blk2", [128, 2], F32, kind="ExternalInput").ap()
    onec_in = nc.dram_tensor("onec", [128, 1], F32, kind="ExternalInput").ap()
    oner_in = nc.dram_tensor("oner", [1, 128], F32, kind="ExternalInput").ap()
    qt_out = nc.dram_tensor("qt", [D, Q], BF16, kind="ExternalOutput").ap()
    kt_out = nc.dram_tensor("kt", [D, Q], BF16, kind="ExternalOutput").ap()
    v_out = nc.dram_tensor("v", [Q, D], BF16, kind="ExternalOutput").ap()

    with contextlib.ExitStack() as st:
        s = Sched(nc)
        s.make_sems(st, ["ld", "ldw", "te", "ve", "se", "ge", "stq"])
        sb = lambda name, shp, dt=F32: st.enter_context(nc.sbuf_tensor(name, shp, dt))
        ps = [st.enter_context(nc.psum_tensor(f"ps{i}", [128, QS], F32)) for i in range(8)]
        xc = [sb("xc0", [128, Q]), sb("xc1", [128, Q])]
        xq = [sb("xq0", [128, Q]), sb("xq1", [128, Q])]
        xa = [sb("xa0", [128, QS]), sb("xa1", [128, QS])]
        stat = (xq[1], xc[1])
        onec = sb("s_onec", [128, 1])
        oner = sb("s_oner", [1, 128])
        qnw = sb("s_qnw", [128, 1])
        knw = sb("s_knw", [128, 1])
        sel2 = sb("s_sel2", [2, 128])
        blk2 = sb("s_blk2", [128, 2])
        xln = sb("xln", [128, NCH * Q], BF16)
        wbuf = sb("wbuf", [128, NCH * 1024], BF16)
        sqs = sb("sqs", [128, QS])
        ms2 = sb("ms2", [2, QS])
        rpsb = sb("rpsb", [128, QS])
        osb = sb("osb", [128, QS], BF16)
        block = st.enter_context(nc.Block())

        for dst, src_ in [(onec, onec_in), (oner, oner_in), (qnw, qnw_in),
                          (knw, knw_in), (sel2, sel2_in), (blk2, blk2_in)]:
            s.dma("sync", dst[:], src_)
        emit_ln(s, lambda ci: x_in[ci * 128:(ci + 1) * 128, :], NCH, Q,
                ps, xc, xq, xa, stat, onec, oner, xln)
        s.barrier()

        def out_qk(dest, wvec):
            def fn(jc, si, pj, q0, qn):
                s.wait("scalar", "te")
                s.wait("scalar", "ve")
                s.emit("scalar", lambda e, pj=pj: e.activation(sqs[:, :], pj[:, :],
                                                               AF.Square), inc="se")
                s.wait("tensor", "se")
                s.emit("tensor", lambda e: e.matmul(ps[6][0:2, :], blk2[:, :], sqs[:, :],
                                                    start=True, stop=True), inc="te")
                s.wait("vector", "te")
                s.emit("vector", lambda e: e.tensor_scalar(ms2[:, :], ps[6][0:2, :],
                                                           1.0 / 64, EPS_RMS,
                                                           ALU.mult, ALU.add), inc="ve")
                s.wait("scalar", "ve")
                s.emit("scalar", lambda e: e.activation(ms2[:, :], ms2[:, :], AF.Sqrt),
                       inc="se")
                s.wait("vector", "se")
                s.emit("vector", lambda e: e.reciprocal(ms2[:, :], ms2[:, :]), inc="ve")
                s.wait("tensor", "ve")
                s.emit("tensor", lambda e: e.matmul(ps[7][:, :], sel2[:, :], ms2[:, :],
                                                    start=True, stop=True), inc="te")
                s.wait("vector", "te")
                s.emit("vector", lambda e: e.tensor_copy(rpsb[:, :], ps[7][:, :]), inc="ve")
                s.wait("vector", "stq")
                s.emit("vector", lambda e, pj=pj, wvec=wvec: e.scalar_tensor_tensor(
                    osb[:, :], pj[:, :], wvec[:, 0:1], rpsb[:, :], ALU.mult, ALU.mult),
                    inc="ve")
                s.wait("gpsimd", "ve")
                s.dma("gpsimd", dest[jc * 128:(jc + 1) * 128, q0:q0 + qn],
                      osb[:, 0:qn], inc="stq")
            return fn
        emit_proj(s, ps, lambda ci: wq_in[ci * 128:(ci + 1) * 128, :], wbuf,
                  xln, out_qk(qt_out, qnw), NCH, NCH, banks=[0, 1, 2, 3, 4, 5])
        s.barrier()
        emit_proj(s, ps, lambda ci: wk_in[ci * 128:(ci + 1) * 128, :], wbuf,
                  xln, out_qk(kt_out, knw), NCH, NCH, banks=[0, 1, 2, 3, 4, 5])
        s.barrier()

        for ci in range(NCH):
            s.dma("sync", wbuf[:, ci * 1024:(ci + 1) * 1024],
                  wv_in[ci * 128:(ci + 1) * 128, :], inc="ldw")
        s.wait("tensor", "ldw")
        bank_snap = {}
        k = 0
        for tt in range(Q // 128):
            for vs in range(2):
                bk = k % 8
                k += 1
                pj = ps[bk]
                if bk in bank_snap:
                    s.wait_snap("tensor", bank_snap[bk])
                for ci in range(NCH):
                    s.emit("tensor", lambda e, pj=pj, tt=tt, vs=vs, ci=ci:
                           e.matmul(pj[:, :], xln[:, ci * Q + tt * 128:ci * Q + (tt + 1) * 128],
                                    wbuf[:, ci * 1024 + vs * QS:ci * 1024 + (vs + 1) * QS],
                                    start=(ci == 0), stop=(ci == NCH - 1)), inc="te")
                s.wait("vector", "te")
                s.wait("vector", "stq")
                s.emit("vector", lambda e, pj=pj: e.tensor_copy(osb[:, :], pj[:, :]),
                       inc="ve")
                s.wait("gpsimd", "ve")
                s.dma("gpsimd", v_out[tt * 128:(tt + 1) * 128, vs * QS:(vs + 1) * QS],
                      osb[:, :], inc="stq")
                bank_snap[bk] = s.snap()
        s.wait("gpsimd", "stq")
        s.build(block)
    return nc


def run_attnpre(xT, wqT_bf, wkT_bf, wvT_bf, qn_w, kn_w):
    nc = _get("attnpre", build_attnpre)
    onec = np.ones((128, 1), np.float32)
    oner = np.ones((1, 128), np.float32)
    qnw = np.ascontiguousarray(np.tile(qn_w, 2)[:, None].astype(np.float32))
    knw = np.ascontiguousarray(np.tile(kn_w, 2)[:, None].astype(np.float32))
    sel2 = np.zeros((2, 128), np.float32)
    sel2[0, 0:64] = 1.0
    sel2[1, 64:128] = 1.0
    blk2 = np.ascontiguousarray(sel2.T)
    in_maps = []
    for c in range(NCORE):
        b_, t0 = c // 4, (c % 4) * Q
        in_maps.append(dict(x=np.ascontiguousarray(xT[b_][:, t0:t0 + Q]),
                            wq=wqT_bf, wk=wkT_bf, wv=wvT_bf, qnw=qnw, knw=knw,
                            sel2=sel2, blk2=blk2, onec=onec, oner=oner))
    res = run_bass_kernel_spmd(nc, in_maps, core_ids=list(range(NCORE)))
    qT = np.empty((B, D, T), BF)
    kT = np.empty((B, D, T), BF)
    v = np.empty((B, T, D), BF)
    for c in range(NCORE):
        b_, t0 = c // 4, (c % 4) * Q
        qT[b_][:, t0:t0 + Q] = res.results[c]["qt"]
        kT[b_][:, t0:t0 + Q] = res.results[c]["kt"]
        v[b_][t0:t0 + Q, :] = res.results[c]["v"]
    return qT, kT, v


def build_attn():
    nc = bass.Bass("TRN2", target_bir_lowering=False)
    qt_in = nc.dram_tensor("qt", [D, Q], BF16, kind="ExternalInput").ap()
    kt_in = nc.dram_tensor("ktf", [D, T], BF16, kind="ExternalInput").ap()
    vre_in = nc.dram_tensor("vre", [NH, 128, T // 128, DH], BF16, kind="ExternalInput").ap()
    x_in = nc.dram_tensor("x", [D, Q], F32, kind="ExternalInput").ap()
    wo_in = nc.dram_tensor("wo", [D, D], BF16, kind="ExternalInput").ap()
    eye_in = nc.dram_tensor("eye", [128, 128], F32, kind="ExternalInput").ap()
    onec_in = nc.dram_tensor("onec", [128, 1], F32, kind="ExternalInput").ap()
    oner_in = nc.dram_tensor("oner", [1, 128], F32, kind="ExternalInput").ap()
    out_t = nc.dram_tensor("out", [D, Q], F32, kind="ExternalOutput").ap()
    NKT = T // 128

    with contextlib.ExitStack() as st:
        s = Sched(nc)
        s.make_sems(st, ["ld", "ldw", "te", "ve", "se", "ge", "stq"])
        sb = lambda name, shp, dt=F32: st.enter_context(nc.sbuf_tensor(name, shp, dt))
        ps = [st.enter_context(nc.psum_tensor(f"ps{i}", [128, QS], F32)) for i in range(8)]
        qsb = sb("qsb", [128, NCH * Q], BF16)
        ksb = [sb(f"ksb{i}", [128, T], BF16) for i in range(2)]
        vsb = [sb(f"vsb{i}", [128, NKT * (DH + 1)], BF16) for i in range(2)]
        otb = sb("otb", [128, NCH * Q], BF16)
        eye = sb("s_eye", [128, 128])
        onec = sb("s_onec", [128, 1])
        oner = sb("s_oner", [1, 128])
        mxt = sb("mxt", [128, 16])
        mx1 = sb("mx1", [128, 1])
        mrow = sb("mrow", [1, QS])
        mrep = sb("mrep", [128, QS])
        scs = [sb(f"scs{i}", [128, QS]) for i in range(2)]
        pbs = [sb(f"pbs{i}", [128, QS], BF16) for i in range(2)]
        rrow = sb("rrow", [1, QS])
        rrep = sb("rrep", [128, QS])
        xa = [sb("xa0", [128, QS]), sb("xa1", [128, QS])]
        wbuf = sb("wbuf", [128, NCH * 1024], BF16)
        tmpv = sb("tmpv", [128, QS])
        block = st.enter_context(nc.Block())

        for dst, src_ in [(eye, eye_in), (onec, onec_in), (oner, oner_in)]:
            s.dma("sync", dst[:], src_)
        for ci in range(NCH):
            s.dma("sync", qsb[:, ci * Q:(ci + 1) * Q], qt_in[ci * 128:(ci + 1) * 128, :])
        s.emit("vector", lambda e: e.memset(
            vsb[0][:, :].rearrange("p (k d) -> p k d", d=DH + 1)[:, :, DH:DH + 1], 1.0))
        s.emit("vector", lambda e: e.memset(
            vsb[1][:, :].rearrange("p (k d) -> p k d", d=DH + 1)[:, :, DH:DH + 1], 1.0),
            inc="ve")
        kv_snap = {0: None, 1: None}
        for h in range(NH):
            hb = h % 2
            jc, hr = h // 2, (h % 2) * 64
            if kv_snap[hb] is not None:
                s.wait_snap("sync", kv_snap[hb], sems=("te",))
            s.dma("sync", ksb[hb][0:64, :], kt_in[h * 64:(h + 1) * 64, :], inc="ldw")
            s.dma("sync", ksb[hb][64:128, :], kt_in[h * 64:(h + 1) * 64, :], inc="ldw")
            v3 = vsb[hb][:, :].rearrange("p (k d) -> p k d", d=DH + 1)
            for g in range(8):
                s.dma("sync", v3[:, g * 8:(g + 1) * 8, 0:DH],
                      vre_in[h, :, g * 8:(g + 1) * 8, :], inc="ldw")
            s.wait("tensor", "ldw")
            for qb in range(NST):
                q0 = jc * Q + qb * QS
                for qt in range(4):
                    lq = qsb[hr:hr + 64, q0 + qt * 128:q0 + (qt + 1) * 128]
                    for ks in range(16):
                        bk = ks % 2
                        s.wait_snap("tensor", {"ve": max(0, s.posted["ve"] - 1)},
                                    sems=("ve",)) if ks >= 2 else None
                        s.emit("tensor", lambda e, lq=lq, bk=bk, ks=ks, hb=hb, hr=hr:
                               e.matmul(ps[bk][:, :], lq,
                                        ksb[hb][hr:hr + 64, ks * QS:(ks + 1) * QS],
                                        start=True, stop=True), inc="te")
                        s.wait("vector", "te")
                        s.emit("vector", lambda e, bk=bk, ks=ks:
                               e.reduce_max(mxt[:, ks:ks + 1], ps[bk][:, :], axis=AX.X),
                               inc="ve")
                    s.wait("vector", "ve")
                    s.emit("vector", lambda e: e.reduce_max(mx1[:, :], mxt[:, :], axis=AX.X),
                           inc="ve")
                    s.wait("tensor", "ve")
                    s.emit("tensor", lambda e: e.transpose(ps[2][0:1, 0:128], mx1[:, :],
                                                           eye[:, :]), inc="te")
                    s.wait("vector", "te")
                    s.emit("vector", lambda e, qt=qt: e.tensor_copy(
                        mrow[0:1, qt * 128:(qt + 1) * 128], ps[2][0:1, 0:128]), inc="ve")
                s.wait("tensor", "ve")
                s.emit("tensor", lambda e: e.matmul(ps[3][:, :], oner[0:1, :], mrow[0:1, :],
                                                    start=True, stop=True), inc="te")
                s.wait("vector", "te")
                s.emit("vector", lambda e: e.tensor_copy(mrep[:, :], ps[3][:, :]), inc="ve")
                rhsq = qsb[hr:hr + 64, q0:q0 + QS]
                pv_te = {}
                for kt in range(NKT):
                    bk = 5 + kt % 2
                    sc = scs[kt % 2]
                    pb = pbs[kt % 2]
                    if kt >= 2:
                        s.wait("tensor", "se", s.posted["se"] - 1)
                    s.emit("tensor", lambda e, kt=kt, bk=bk, rhsq=rhsq, hb=hb, hr=hr:
                           e.matmul(ps[bk][:, :], ksb[hb][hr:hr + 64, kt * 128:(kt + 1) * 128],
                                    rhsq, start=True, stop=True), inc="te")
                    s.wait("vector", "te")
                    if kt >= 2:
                        s.wait("vector", "se", s.posted["se"] - 1)
                    s.emit("vector", lambda e, bk=bk, sc=sc: e.tensor_sub(
                        sc[:, :], ps[bk][:, :], mrep[:, :]), inc="ve")
                    s.wait("scalar", "ve")
                    if kt >= 2:
                        s.wait("scalar", "te", pv_te[kt - 2])
                    s.emit("scalar", lambda e, sc=sc, pb=pb: e.activation(
                        pb[:, :], sc[:, :], AF.Exp, scale=0.125), inc="se")
                    s.wait("tensor", "se")
                    s.emit("tensor", lambda e, kt=kt, pb=pb, hb=hb:
                           e.matmul(ps[4][0:DH + 1, :],
                                    vsb[hb][:, kt * (DH + 1):(kt + 1) * (DH + 1)],
                                    pb[:, :], start=(kt == 0), stop=(kt == NKT - 1)),
                           inc="te")
                    pv_te[kt] = s.posted["te"]
                s.wait("vector", "te")
                s.emit("vector", lambda e: e.reciprocal(rrow[0:1, :], ps[4][64:65, :]),
                       inc="ve")
                s.wait("tensor", "ve")
                s.emit("tensor", lambda e: e.matmul(ps[7][:, :], oner[0:1, :], rrow[0:1, :],
                                                    start=True, stop=True), inc="te")
                s.wait("vector", "te")
                s.emit("vector", lambda e: e.tensor_copy(rrep[:, :], ps[7][:, :]), inc="ve")
                s.emit("vector", lambda e, hr=hr, q0=q0: e.tensor_mul(
                    otb[hr:hr + 64, q0:q0 + QS], ps[4][0:DH, :], rrep[0:DH, :]), inc="ve")
            kv_snap[hb] = {"te": s.posted["te"]}
        s.barrier()

        def out_wo(jc, si, pj, q0, qn):
            bi = (jc * NST + si) % 2
            if jc * NST + si >= 2:
                s.wait("sync", "ve")
            s.dma("sync", xa[bi][:, 0:qn], x_in[jc * 128:(jc + 1) * 128, q0:q0 + qn])
            s.wait("vector", "te")
            s.wait("vector", "ld")
            s.wait("vector", "stq")
            s.emit("vector", lambda e, pj=pj, bi=bi, qn=qn: e.tensor_add(
                tmpv[:, 0:qn], pj[:, 0:qn], xa[bi][:, 0:qn]), inc="ve")
            s.wait("gpsimd", "ve")
            s.dma("gpsimd", out_t[jc * 128:(jc + 1) * 128, q0:q0 + qn], tmpv[:, 0:qn],
                  inc="stq")
        emit_proj(s, ps, lambda ci: wo_in[ci * 128:(ci + 1) * 128, :], wbuf,
                  otb, out_wo, NCH, NCH, banks=[0, 1, 2, 3, 5, 6])
        s.wait("gpsimd", "stq")
        s.build(block)
    return nc


def run_attn(x1T, qT, kT, v, woT_bf):
    nc = _get("attn", build_attn)
    eye = np.eye(128, dtype=np.float32)
    onec = np.ones((128, 1), np.float32)
    oner = np.ones((1, 128), np.float32)
    vre = np.ascontiguousarray(
        v.reshape(B, T // 128, 128, NH, DH).transpose(0, 3, 2, 1, 4))
    in_maps = []
    for c in range(NCORE):
        b_, t0 = c // 4, (c % 4) * Q
        in_maps.append(dict(qt=np.ascontiguousarray(qT[b_][:, t0:t0 + Q]),
                            ktf=kT[b_], vre=vre[b_],
                            x=np.ascontiguousarray(x1T[b_][:, t0:t0 + Q]),
                            wo=woT_bf, eye=eye, onec=onec, oner=oner))
    res = run_bass_kernel_spmd(nc, in_maps, core_ids=list(range(NCORE)))
    out = np.empty((B, D, T), np.float32)
    for c in range(NCORE):
        b_, t0 = c // 4, (c % 4) * Q
        out[b_][:, t0:t0 + Q] = res.results[c]["out"]
    return out


def build_s6pre():
    nc = bass.Bass("TRN2", target_bir_lowering=False)
    x_in = nc.dram_tensor("x", [D, Q], F32, kind="ExternalInput").ap()
    wi_in = nc.dram_tensor("wi", [D, 2 * DIN], BF16, kind="ExternalInput").ap()
    wx_in = nc.dram_tensor("wx", [DIN, 2 * DST + DIN], BF16, kind="ExternalInput").ap()
    wdt_in = nc.dram_tensor("wdt", [DST, DIN], BF16, kind="ExternalInput").ap()
    bi_in = nc.dram_tensor("bi", [128, 32], F32, kind="ExternalInput").ap()
    bx_in = nc.dram_tensor("bx", [128, 17], F32, kind="ExternalInput").ap()
    bdt_in = nc.dram_tensor("bdt", [128, 16], F32, kind="ExternalInput").ap()
    onec_in = nc.dram_tensor("onec", [128, 1], F32, kind="ExternalInput").ap()
    oner_in = nc.dram_tensor("oner", [1, 128], F32, kind="ExternalInput").ap()
    xs_out = nc.dram_tensor("xs", [DIN, Q], F32, kind="ExternalOutput").ap()
    res_out = nc.dram_tensor("res", [DIN, Q], F32, kind="ExternalOutput").ap()
    dl_out = nc.dram_tensor("dl", [DIN, Q], F32, kind="ExternalOutput").ap()
    bm_out = nc.dram_tensor("bm", [DST, Q], F32, kind="ExternalOutput").ap()
    cm_out = nc.dram_tensor("cm", [DIN, Q], F32, kind="ExternalOutput").ap()
    xln_out = nc.dram_tensor("xlno", [D, Q], BF16, kind="ExternalOutput").ap()

    with contextlib.ExitStack() as st:
        s = Sched(nc)
        s.make_sems(st, ["ld", "ldw", "te", "ve", "se", "ge", "stq"])
        sb = lambda name, shp, dt=F32: st.enter_context(nc.sbuf_tensor(name, shp, dt))
        ps = [st.enter_context(nc.psum_tensor(f"ps{i}", [128, QS], F32)) for i in range(8)]
        xc = [sb("xc0", [128, Q]), sb("xc1", [128, Q])]
        xq = [sb("xq0", [128, Q]), sb("xq1", [128, Q])]
        xa = [sb("xa0", [128, QS]), sb("xa1", [128, QS])]
        stat = (xq[1], xc[1])
        onec = sb("s_onec", [128, 1])
        oner = sb("s_oner", [1, 128])
        bisb = sb("s_bi", [128, 32])
        bxsb = sb("s_bx", [128, 17])
        bdtsb = sb("s_bdt", [128, 16])
        xln = sb("xln", [128, NCH * Q], BF16)
        xssb = sb("xssb", [128, 16 * Q], BF16)
        drawsb = sb("drawsb", [64, Q], BF16)
        wsm = [sb(f"wsm{i}", [128, 1024], BF16) for i in range(2)]
        tv = [sb(f"tv{i}", [128, QS]) for i in range(4)]
        block = st.enter_context(nc.Block())

        for dst, src_ in [(onec, onec_in), (oner, oner_in), (bisb, bi_in),
                          (bxsb, bx_in), (bdtsb, bdt_in)]:
            s.dma("sync", dst[:], src_)
        emit_ln(s, lambda ci: x_in[ci * 128:(ci + 1) * 128, :], NCH, Q,
                ps, xc, xq, xa, stat, onec, oner, xln)
        s.barrier()
        for ci in range(NCH):
            s.dma("gpsimd", xln_out[ci * 128:(ci + 1) * 128, :], xln[:, ci * Q:(ci + 1) * Q],
                  inc="stq")

        bank_snap = {}
        wsnap = {0: None, 1: None}

        def stream_proj(w_src_fn, rhs_sb, rhs_nch, njc, out_fn, rhs_cs=Q):
            ngrp = (njc + 7) // 8
            for si in range(NST):
                q0 = si * QS
                for jg in range(ngrp):
                    nj = min(8, njc - jg * 8)
                    for ci in range(rhs_nch):
                        bi = ci % 2
                        if wsnap[bi] is not None:
                            s.wait_snap("sync", wsnap[bi], sems=("te",))
                        s.dma("sync", wsm[bi][:, 0:nj * 128], w_src_fn(ci, jg, nj),
                              inc="ldw")
                        s.wait("tensor", "ldw")
                        for j in range(nj):
                            if ci == 0 and bank_snap.get(j) is not None:
                                s.wait_snap("tensor", bank_snap[j])
                            s.emit("tensor", lambda e, bi=bi, j=j, ci=ci, q0=q0:
                                   e.matmul(ps[j][:, :], wsm[bi][:, j * 128:(j + 1) * 128],
                                            rhs_sb[:, ci * rhs_cs + q0:ci * rhs_cs + q0 + QS],
                                            start=(ci == 0), stop=(ci == rhs_nch - 1)),
                                   inc="te")
                        wsnap[bi] = {"te": s.posted["te"]}
                    for j in range(nj):
                        out_fn(jg * 8 + j, j, si, q0)
                        bank_snap[j] = s.snap()

        def out_wi(jc, j, si, q0):
            if jc < 16:
                s.wait("scalar", "te")
                s.wait("scalar", "ve")
                s.wait("scalar", "stq")
                s.emit("scalar", lambda e, j=j, jc=jc, q0=q0: e.activation(
                    xssb[:, jc * Q + q0:jc * Q + q0 + QS], ps[j][:, :], AF.Silu,
                    bias=bisb[:, jc:jc + 1]))
                s.emit("scalar", lambda e, j=j, jc=jc: e.activation(
                    tv[0][:, :], ps[j][:, :], AF.Silu, bias=bisb[:, jc:jc + 1]),
                    inc="se")
                s.wait("gpsimd", "se")
                s.dma("gpsimd", xs_out[jc * 128:(jc + 1) * 128, q0:q0 + QS],
                      tv[0][:, :], inc="stq")
            else:
                s.wait("vector", "te")
                s.wait("vector", "stq")
                s.emit("vector", lambda e, j=j, jc=jc: e.tensor_scalar(
                    tv[1][:, :], ps[j][:, :], bisb[:, jc:jc + 1], None, ALU.add),
                    inc="ve")
                s.wait("gpsimd", "ve")
                s.dma("gpsimd", res_out[(jc - 16) * 128:(jc - 15) * 128, q0:q0 + QS],
                      tv[1][:, :], inc="stq")
        stream_proj(lambda ci, jg, nj: wi_in[ci * 128:(ci + 1) * 128,
                                             jg * 1024:jg * 1024 + nj * 128],
                    xln, NCH, 32, out_wi)
        s.barrier()

        def out_wx(jc, j, si, q0):
            s.wait("vector", "te")
            s.wait("vector", "stq")
            if jc == 0:
                s.emit("vector", lambda e, j=j: e.tensor_scalar(
                    tv[2][:, :], ps[j][:, :], bxsb[:, 0:1], None, ALU.add), inc="ve")
                s.emit("vector", lambda e, q0=q0: e.tensor_copy(
                    drawsb[0:64, q0:q0 + QS], tv[2][0:64, :]), inc="ve")
                s.wait("gpsimd", "ve")
                s.dma("gpsimd", bm_out[:, q0:q0 + QS], tv[2][64:128, :], inc="stq")
            else:
                s.emit("vector", lambda e, j=j, jc=jc: e.tensor_scalar(
                    tv[2][:, :], ps[j][:, :], bxsb[:, jc:jc + 1], None, ALU.add),
                    inc="ve")
                s.wait("gpsimd", "ve")
                s.dma("gpsimd", cm_out[(jc - 1) * 128:jc * 128, q0:q0 + QS],
                      tv[2][:, :], inc="stq")
        stream_proj(lambda ci, jg, nj: wx_in[ci * 128:(ci + 1) * 128,
                                             jg * 1024:jg * 1024 + nj * 128],
                    xssb, 16, 17, out_wx)
        s.barrier()

        s.dma("sync", wsm[0][0:64, :], wdt_in[:, 0:1024], inc="ldw")
        s.dma("sync", wsm[1][0:64, :], wdt_in[:, 1024:2048], inc="ldw")
        s.wait("tensor", "ldw")
        for jc in range(16):
            wslice = (wsm[0] if jc < 8 else wsm[1])
            wof = (jc % 8) * 128
            for si in range(NST):
                q0 = si * QS
                bk = (jc * NST + si) % 4
                if bank_snap.get(bk) is not None:
                    s.wait_snap("tensor", bank_snap[bk])
                s.emit("tensor", lambda e, bk=bk, wslice=wslice, wof=wof, q0=q0:
                       e.matmul(ps[bk][:, :], wslice[0:64, wof:wof + 128],
                                drawsb[0:64, q0:q0 + QS], start=True, stop=True),
                       inc="te")
                s.wait("vector", "te")
                s.wait("vector", "se")
                s.wait("vector", "stq")
                s.emit("vector", lambda e, bk=bk, jc=jc: e.tensor_scalar(
                    tv[0][:, :], ps[bk][:, :], bdtsb[:, jc:jc + 1], None, ALU.add),
                    inc="ve")
                s.wait("scalar", "ve")
                s.emit("scalar", lambda e: e.activation(tv[1][:, :], tv[0][:, :], AF.Abs))
                s.emit("scalar", lambda e: e.activation(tv[1][:, :], tv[1][:, :], AF.Exp,
                                                        scale=-1.0), inc="se")
                s.wait("vector", "se")
                s.emit("vector", lambda e: e.tensor_scalar_add(tv[1][:, :], tv[1][:, :], 1.0),
                       inc="ve")
                s.wait("scalar", "ve")
                s.emit("scalar", lambda e: e.activation(tv[1][:, :], tv[1][:, :], AF.Ln),
                       inc="se")
                s.wait("vector", "se")
                s.emit("vector", lambda e: e.tensor_scalar_max(tv[2][:, :], tv[0][:, :], 0.0))
                s.emit("vector", lambda e: e.tensor_add(tv[2][:, :], tv[2][:, :], tv[1][:, :]),
                       inc="ve")
                s.wait("gpsimd", "ve")
                s.dma("gpsimd", dl_out[jc * 128:(jc + 1) * 128, q0:q0 + QS],
                      tv[2][:, :], inc="stq")
                bank_snap[bk] = s.snap()
        s.wait("gpsimd", "stq")
        s.build(block)
    return nc


TSC = 128
NTC = T // TSC


def build_scan():
    nc = bass.Bass("TRN2", target_bir_lowering=False)
    dl_in = nc.dram_tensor("dl", [512, T], F32, kind="ExternalInput").ap()
    xs_in = nc.dram_tensor("xs", [512, T], F32, kind="ExternalInput").ap()
    bf_in = nc.dram_tensor("bf", [NTC, DST * TSC], F32, kind="ExternalInput").ap()
    al_in = nc.dram_tensor("al", [512, DST], F32, kind="ExternalInput").ap()
    oner_in = nc.dram_tensor("oner", [1, 128], F32, kind="ExternalInput").ap()
    y_out = nc.dram_tensor("y", [512, T], F32, kind="ExternalOutput").ap()

    with contextlib.ExitStack() as st:
        s = Sched(nc)
        s.make_sems(st, ["ld", "ldw", "te", "ve", "se", "ge", "stq"])
        sb = lambda name, shp, dt=F32: st.enter_context(nc.sbuf_tensor(name, shp, dt))
        ps = [st.enter_context(nc.psum_tensor(f"ps{i}", [128, QS], F32)) for i in range(8)]
        an = sb("an", [128, 4 * DST])
        oner = sb("s_oner", [1, 128])
        brow = sb("brow", [1, DST * TSC])
        brep = sb("brep", [128, DST * TSC])
        dch = [sb(f"dch{i}", [128, TSC]) for i in range(2)]
        xch = [sb(f"xch{i}", [128, TSC]) for i in range(2)]
        wd = sb("wd", [128, TSC])
        ec = sb("ec", [128, DST * TSC])
        dac = sb("dac", [128, DST * TSC])
        uc = sb("uc", [128, DST * TSC])
        hc = sb("hc", [128, DST * TSC])
        carry = sb("carry", [128, 4 * DST])
        yc = sb("yc", [128, TSC])
        block = st.enter_context(nc.Block())

        s.dma("sync", oner[:], oner_in)
        for dt in range(4):
            s.dma("sync", an[:, dt * DST:(dt + 1) * DST], al_in[dt * 128:(dt + 1) * 128, :])
        s.wait("scalar", "ld")
        s.emit("scalar", lambda e: e.activation(an[:, :], an[:, :], AF.Exp), inc="se")
        s.wait("vector", "se")
        s.emit("vector", lambda e: e.tensor_scalar_mul(an[:, :], an[:, :], -1.0), inc="ve")
        s.emit("vector", lambda e: e.memset(carry[:, :], 0.0), inc="ve")
        s.barrier()

        hview = hc[:, :].rearrange("p (n t) -> p n t", t=TSC)
        for tc in range(NTC):
            s.wait("sync", "te")
            s.dma("sync", brow[0:1, :], bf_in[tc:tc + 1, :])
            s.wait("tensor", "ld")
            s.wait("tensor", "ve")
            bsnap = {}
            for sp in range(DST * TSC // QS):
                if sp >= 8:
                    s.wait("tensor", "ve", bsnap[sp - 8])
                s.emit("tensor", lambda e, sp=sp: e.matmul(
                    ps[sp % 8][:, :], oner[0:1, :], brow[0:1, sp * QS:(sp + 1) * QS],
                    start=True, stop=True), inc="te")
                s.wait("vector", "te")
                s.emit("vector", lambda e, sp=sp: e.tensor_copy(
                    brep[:, sp * QS:(sp + 1) * QS], ps[sp % 8][:, :]), inc="ve")
                bsnap[sp] = s.posted["ve"]
            for dt in range(4):
                bi = dt % 2
                s.wait("sync", "ve")
                s.dma("sync", dch[bi][:, :], dl_in[dt * 128:(dt + 1) * 128,
                                                   tc * TSC:(tc + 1) * TSC])
                s.dma("sync", xch[bi][:, :], xs_in[dt * 128:(dt + 1) * 128,
                                                   tc * TSC:(tc + 1) * TSC])
                s.wait("vector", "ld")
                s.wait("vector", "se")
                s.emit("vector", lambda e, bi=bi: e.tensor_mul(wd[:, :], dch[bi][:, :],
                                                               xch[bi][:, :]), inc="ve")
                s.emit("vector", lambda e, bi=bi, dt=dt: e.tensor_tensor(
                    ec[:, :].rearrange("p (n t) -> p n t", t=TSC),
                    an[:, dt * DST:(dt + 1) * DST].rearrange("p (n u) -> p n u", u=1)
                      .broadcast_to([128, DST, TSC]),
                    dch[bi][:, :].rearrange("p (u t) -> p u t", u=1)
                      .broadcast_to([128, DST, TSC]),
                    ALU.mult), inc="ve")
                s.wait("scalar", "ve")
                s.emit("scalar", lambda e: e.activation(dac[:, :], ec[:, :], AF.Exp),
                       inc="se")
                s.emit("vector", lambda e: e.tensor_tensor(
                    uc[:, :].rearrange("p (n t) -> p n t", t=TSC),
                    brep[:, :].rearrange("p (n t) -> p n t", t=TSC),
                    wd[:, :].rearrange("p (u t) -> p u t", u=1)
                      .broadcast_to([128, DST, TSC]),
                    ALU.mult), inc="ve")
                s.wait("vector", "se")
                for n in range(DST):
                    s.emit("vector", lambda e, n=n, dt=dt: e.tensor_tensor_scan(
                        hc[:, n * TSC:(n + 1) * TSC], dac[:, n * TSC:(n + 1) * TSC],
                        uc[:, n * TSC:(n + 1) * TSC], carry[:, dt * DST + n:dt * DST + n + 1],
                        ALU.mult, ALU.add), inc="ve")
                s.emit("vector", lambda e, dt=dt: e.tensor_copy(
                    carry[:, dt * DST:(dt + 1) * DST], hview[:, :, TSC - 1]), inc="ve")
                w_ = DST
                while w_ > 1:
                    h_ = w_ // 2
                    s.emit("vector", lambda e, h_=h_: e.tensor_add(
                        hview[:, 0:h_, :], hview[:, 0:h_, :], hview[:, h_:2 * h_, :]),
                        inc="ve")
                    w_ = h_
                s.wait("vector", "stq")
                s.emit("vector", lambda e: e.tensor_copy(yc[:, :], hc[:, 0:TSC]), inc="ve")
                s.wait("gpsimd", "ve")
                s.dma("gpsimd", y_out[dt * 128:(dt + 1) * 128, tc * TSC:(tc + 1) * TSC],
                      yc[:, :], inc="stq")
        s.wait("gpsimd", "stq")
        s.build(block)
    return nc


def build_s6post():
    nc = bass.Bass("TRN2", target_bir_lowering=False)
    y_in = nc.dram_tensor("y", [DIN, Q], F32, kind="ExternalInput").ap()
    xs_in = nc.dram_tensor("xs", [DIN, Q], F32, kind="ExternalInput").ap()
    res_in = nc.dram_tensor("res", [DIN, Q], F32, kind="ExternalInput").ap()
    cm_in = nc.dram_tensor("cm", [DIN, Q], F32, kind="ExternalInput").ap()
    x_in = nc.dram_tensor("x", [D, Q], F32, kind="ExternalInput").ap()
    resd_in = nc.dram_tensor("resd", [D, Q], BF16, kind="ExternalInput").ap()
    wo_in = nc.dram_tensor("wo", [DIN, D], BF16, kind="ExternalInput").ap()
    dp_in = nc.dram_tensor("dp", [128, 16], F32, kind="ExternalInput").ap()
    bo_in = nc.dram_tensor("bo", [128, NCH], F32, kind="ExternalInput").ap()
    ga_in = nc.dram_tensor("ga", [128, NCH], F32, kind="ExternalInput").ap()
    be_in = nc.dram_tensor("be", [128, NCH], F32, kind="ExternalInput").ap()
    onec_in = nc.dram_tensor("onec", [128, 1], F32, kind="ExternalInput").ap()
    oner_in = nc.dram_tensor("oner", [1, 128], F32, kind="ExternalInput").ap()
    out_t = nc.dram_tensor("out", [D, Q], F32, kind="ExternalOutput").ap()

    with contextlib.ExitStack() as st:
        s = Sched(nc)
        s.make_sems(st, ["ld", "ldw", "te", "ve", "se", "ge", "stq"])
        sb = lambda name, shp, dt=F32: st.enter_context(nc.sbuf_tensor(name, shp, dt))
        ps = [st.enter_context(nc.psum_tensor(f"ps{i}", [128, QS], F32)) for i in range(8)]
        onec = sb("s_onec", [128, 1])
        oner = sb("s_oner", [1, 128])
        dpsb = sb("s_dp", [128, 16])
        bosb = sb("s_bo", [128, NCH])
        gasb = sb("s_ga", [128, NCH])
        besb = sb("s_be", [128, NCH])
        yf = sb("yf", [128, 16 * Q], BF16)
        zsb = sb("zsb", [128, NCH * Q])
        wbuf = sb("wbuf", [128, 16 * 1024], BF16)
        t1 = [sb(f"t1{i}", [128, Q]) for i in range(2)]
        t2 = [sb(f"t2{i}", [128, Q]) for i in range(2)]
        xq2 = [sb("xq20", [128, Q]), t2[1]]
        xa = [sb("xa0", [128, QS]), sb("xa1", [128, QS])]
        rb = [sb(f"rb{i}", [128, QS], BF16) for i in range(2)]
        xb = [t1[0][:, 0:QS], t1[0][:, QS:2 * QS]]
        block = st.enter_context(nc.Block())

        for dst, src_ in [(onec, onec_in), (oner, oner_in), (dpsb, dp_in),
                          (bosb, bo_in), (gasb, ga_in), (besb, be_in)]:
            s.dma("sync", dst[:], src_)
        for hc in range(16):
            bi = hc % 2
            if hc >= 2:
                s.wait("sync", "ve")
            s.dma("sync", t1[bi][:, :], y_in[hc * 128:(hc + 1) * 128, :])
            s.dma("sync", t2[bi][:, :], cm_in[hc * 128:(hc + 1) * 128, :])
            s.wait("vector", "ld")
            s.emit("vector", lambda e, bi=bi: e.tensor_mul(t1[bi][:, :], t1[bi][:, :],
                                                           t2[bi][:, :]), inc="ve")
            s.wait("sync", "ve")
            s.dma("sync", t2[bi][:, :], xs_in[hc * 128:(hc + 1) * 128, :])
            s.wait("vector", "ld")
            s.emit("vector", lambda e, bi=bi, hc=hc: e.scalar_tensor_tensor(
                t1[bi][:, :], t2[bi][:, :], dpsb[:, hc:hc + 1], t1[bi][:, :],
                ALU.mult, ALU.add), inc="ve")
            s.wait("sync", "ve")
            s.dma("sync", t2[bi][:, :], res_in[hc * 128:(hc + 1) * 128, :])
            s.wait("scalar", "ld")
            s.wait("scalar", "ve")
            s.emit("scalar", lambda e, bi=bi: e.activation(t2[bi][:, :], t2[bi][:, :],
                                                           AF.Silu), inc="se")
            s.wait("vector", "se")
            s.emit("vector", lambda e, bi=bi, hc=hc: e.tensor_mul(
                yf[:, hc * Q:(hc + 1) * Q], t1[bi][:, :], t2[bi][:, :]), inc="ve")
        s.barrier()

        def out_z(jc, si, pj, q0, qn):
            bi = (jc * NST + si) % 2
            if jc * NST + si >= 2:
                s.wait("sync", "ve")
            s.dma("sync", rb[bi][:, 0:qn], resd_in[jc * 128:(jc + 1) * 128, q0:q0 + qn])
            s.wait("vector", "te")
            s.wait("vector", "ld")
            s.emit("vector", lambda e, pj=pj, bi=bi, jc=jc, qn=qn: e.scalar_tensor_tensor(
                t1[0][:, 0:qn], pj[:, 0:qn], bosb[:, jc:jc + 1], rb[bi][:, 0:qn],
                ALU.add, ALU.add))
            s.emit("vector", lambda e, jc=jc, q0=q0, qn=qn: e.tensor_copy(
                zsb[:, jc * Q + q0:jc * Q + q0 + qn], t1[0][:, 0:qn]), inc="ve")
        emit_proj(s, ps, lambda ci: wo_in[ci * 128:(ci + 1) * 128, :], wbuf,
                  yf, out_z, 16, NCH, banks=[0, 1, 2, 3, 4, 5])
        s.barrier()

        nfeat = float(D)
        for ci in range(NCH):
            bi = ci % 2
            if ci >= 2:
                s.wait("scalar", "te")
            s.wait("scalar", "ve")
            s.emit("scalar", lambda e, ci=ci, bi=bi: e.activation(
                xq2[bi][:, :], zsb[:, ci * Q:(ci + 1) * Q], AF.Square), inc="se")
            s.wait("tensor", "se")
            for si in range(NST):
                s.emit("tensor", lambda e, ci=ci, si=si: e.matmul(
                    ps[si][0:1, :], onec[:, 0:1], zsb[:, ci * Q + si * QS:ci * Q + (si + 1) * QS],
                    start=(ci == 0), stop=(ci == NCH - 1)), inc="te")
                s.emit("tensor", lambda e, ci=ci, si=si, bi=bi: e.matmul(
                    ps[4 + si][0:1, :], onec[:, 0:1], xq2[bi][:, si * QS:(si + 1) * QS],
                    start=(ci == 0), stop=(ci == NCH - 1)), inc="te")
        mu = t1[0]
        ms = t1[1]
        s.wait("vector", "te")
        for si in range(NST):
            s.emit("vector", lambda e, si=si: e.tensor_scalar_mul(
                mu[0:1, si * QS:(si + 1) * QS], ps[si][0:1, :], 1.0 / nfeat))
            s.emit("vector", lambda e, si=si: e.tensor_scalar_mul(
                ms[0:1, si * QS:(si + 1) * QS], ps[4 + si][0:1, :], 1.0 / nfeat), inc="ve")
        s.emit("vector", lambda e: e.tensor_mul(t2[0][0:1, :], mu[0:1, :], mu[0:1, :]))
        s.emit("vector", lambda e: e.tensor_sub(ms[0:1, :], ms[0:1, :], t2[0][0:1, :]))
        s.emit("vector", lambda e: e.tensor_scalar_add(ms[0:1, :], ms[0:1, :], EPS_LN),
               inc="ve")
        s.wait("scalar", "ve")
        s.emit("scalar", lambda e: e.activation(t2[1][0:1, :], ms[0:1, :], AF.Sqrt),
               inc="se")
        s.wait("vector", "se")
        s.emit("vector", lambda e: e.reciprocal(t2[1][0:1, :], t2[1][0:1, :]), inc="ve")
        s.emit("vector", lambda e: e.tensor_mul(t2[0][0:1, :], mu[0:1, :], t2[1][0:1, :]),
               inc="ve")
        for si in range(NST):
            q0 = si * QS
            s.wait("tensor", "ve")
            s.emit("tensor", lambda e, q0=q0: e.matmul(ps[0][:, :], oner[0:1, :],
                                                       t2[1][0:1, q0:q0 + QS],
                                                       start=True, stop=True), inc="te")
            s.emit("tensor", lambda e, q0=q0: e.matmul(ps[1][:, :], oner[0:1, :],
                                                       t2[0][0:1, q0:q0 + QS],
                                                       start=True, stop=True), inc="te")
            for ci in range(NCH):
                bi = ci % 2
                if si > 0 or ci >= 2:
                    s.wait("sync", "ve")
                s.dma("sync", xb[bi], x_in[ci * 128:(ci + 1) * 128, q0:q0 + QS])
                s.wait("vector", "te")
                s.wait("vector", "ld")
                s.wait("vector", "stq")
                s.emit("vector", lambda e, ci=ci, q0=q0: e.tensor_mul(
                    xa[0][:, :], zsb[:, ci * Q + q0:ci * Q + q0 + QS], ps[0][:, :]))
                s.emit("vector", lambda e: e.tensor_sub(xa[0][:, :], xa[0][:, :],
                                                        ps[1][:, :]))
                s.emit("vector", lambda e, ci=ci: e.tensor_scalar(
                    xa[1][:, :], xa[0][:, :], gasb[:, ci:ci + 1], besb[:, ci:ci + 1],
                    ALU.mult, ALU.add))
                s.emit("vector", lambda e, bi=bi: e.tensor_add(
                    xa[1][:, :], xa[1][:, :], xb[bi]), inc="ve")
                s.wait("gpsimd", "ve")
                s.dma("gpsimd", out_t[ci * 128:(ci + 1) * 128, q0:q0 + QS],
                      xa[1][:, :], inc="stq")
        s.wait("gpsimd", "stq")
        s.build(block)
    return nc


def run_s6(x4T, wi_bf, wx_bf, wdt_bf, wo_bf, bi, bx, bdt, A_log, Dp, bo, gamma, beta):
    onec = np.ones((128, 1), np.float32)
    oner = np.ones((1, 128), np.float32)
    rs = lambda v, n: np.ascontiguousarray(v.reshape(n, 128).T)
    nc = _get("s6pre", build_s6pre)
    in_maps = []
    for c in range(NCORE):
        b_, t0 = c // 4, (c % 4) * Q
        in_maps.append(dict(x=np.ascontiguousarray(x4T[b_][:, t0:t0 + Q]),
                            wi=wi_bf, wx=wx_bf, wdt=wdt_bf,
                            bi=rs(bi, 32), bx=rs(bx, 17), bdt=rs(bdt, 16),
                            onec=onec, oner=oner))
    res = run_bass_kernel_spmd(nc, in_maps, core_ids=list(range(NCORE)))
    xs = np.empty((B, DIN, T), np.float32)
    rr = np.empty((B, DIN, T), np.float32)
    dl = np.empty((B, DIN, T), np.float32)
    bm = np.empty((B, DST, T), np.float32)
    cm = np.empty((B, DIN, T), np.float32)
    xlnA = np.empty((B, D, T), BF)
    for c in range(NCORE):
        b_, t0 = c // 4, (c % 4) * Q
        r = res.results[c]
        xs[b_][:, t0:t0 + Q] = r["xs"]
        rr[b_][:, t0:t0 + Q] = r["res"]
        dl[b_][:, t0:t0 + Q] = r["dl"]
        bm[b_][:, t0:t0 + Q] = r["bm"]
        cm[b_][:, t0:t0 + Q] = r["cm"]
        xlnA[b_][:, t0:t0 + Q] = r["xlno"]
    nc = _get("scan", build_scan)
    in_maps = []
    for c in range(NCORE):
        b_, d0 = c // 4, (c % 4) * 512
        bfr = np.ascontiguousarray(
            bm[b_].reshape(DST, NTC, TSC).transpose(1, 0, 2).reshape(NTC, DST * TSC))
        in_maps.append(dict(dl=np.ascontiguousarray(dl[b_][d0:d0 + 512]),
                            xs=np.ascontiguousarray(xs[b_][d0:d0 + 512]),
                            bf=bfr, al=np.ascontiguousarray(A_log[d0:d0 + 512]),
                            oner=oner))
    res = run_bass_kernel_spmd(nc, in_maps, core_ids=list(range(NCORE)))
    y = np.empty((B, DIN, T), np.float32)
    for c in range(NCORE):
        b_, d0 = c // 4, (c % 4) * 512
        y[b_][d0:d0 + 512] = res.results[c]["y"]
    nc = _get("s6post", build_s6post)
    in_maps = []
    for c in range(NCORE):
        b_, t0 = c // 4, (c % 4) * Q
        in_maps.append(dict(y=np.ascontiguousarray(y[b_][:, t0:t0 + Q]),
                            xs=np.ascontiguousarray(xs[b_][:, t0:t0 + Q]),
                            res=np.ascontiguousarray(rr[b_][:, t0:t0 + Q]),
                            cm=np.ascontiguousarray(cm[b_][:, t0:t0 + Q]),
                            x=np.ascontiguousarray(x4T[b_][:, t0:t0 + Q]),
                            resd=np.ascontiguousarray(xlnA[b_][:, t0:t0 + Q]),
                            wo=wo_bf, dp=rs(Dp, 16), bo=rs(bo, NCH),
                            ga=rs(gamma, NCH), be=rs(beta, NCH),
                            onec=onec, oner=oner))
    res = run_bass_kernel_spmd(nc, in_maps, core_ids=list(range(NCORE)))
    out = np.empty((B, D, T), np.float32)
    for c in range(NCORE):
        b_, t0 = c // 4, (c % 4) * Q
        out[b_][:, t0:t0 + Q] = res.results[c]["out"]
    return out


def kernel(L0_attn_x, L0_attn_Wq, L0_attn_Wk, L0_attn_Wv, L0_attn_Wo,
           L0_attn_qn_w, L0_attn_kn_w, L0_ffn_gate_w, L0_ffn_gate_b,
           L0_ffn_w1, L0_ffn_b1, L0_ffn_w2, L0_ffn_b2, L0_ffn_ws1, L0_ffn_bs1,
           L0_ffn_ws2, L0_ffn_bs2, L1_attn_Wi, L1_attn_bi, L1_attn_Wx,
           L1_attn_bx, L1_attn_Wdt, L1_attn_bdt, L1_attn_A_log, L1_attn_Dp,
           L1_attn_Wo, L1_attn_bo, L1_attn_gamma, L1_attn_beta, L1_ffn_W_r,
           L1_ffn_W_k, L1_ffn_W_v, L1_ffn_mu_r, L1_ffn_mu_k):
    asnp = lambda a: np.ascontiguousarray(np.asarray(a))
    xT = asnp(np.asarray(L0_attn_x).transpose(0, 2, 1))
    (w1b, w2b, ws1b, ws2b, wqT, wkT, wvT, woT, wrT, wkrT, wvrT,
     wib, wxb, wdtb, wosb) = cast_weights_bf16([
        asnp(L0_ffn_w1), asnp(L0_ffn_w2), asnp(L0_ffn_ws1), asnp(L0_ffn_ws2),
        asnp(np.asarray(L0_attn_Wq).T), asnp(np.asarray(L0_attn_Wk).T),
        asnp(np.asarray(L0_attn_Wv).T), asnp(np.asarray(L0_attn_Wo).T),
        asnp(np.asarray(L1_ffn_W_r).T), asnp(np.asarray(L1_ffn_W_k).T),
        asnp(np.asarray(L1_ffn_W_v).T),
        asnp(L1_attn_Wi), asnp(L1_attn_Wx), asnp(L1_attn_Wdt), asnp(L1_attn_Wo)])
    moe_args = (w1b, w2b, ws1b, ws2b, asnp(L0_ffn_gate_w), asnp(L0_ffn_gate_b),
                asnp(L0_ffn_b1), asnp(L0_ffn_b2), asnp(L0_ffn_bs1),
                asnp(L0_ffn_bs2))
    x1 = run_moe(xT, *moe_args)
    qT, kT, v = run_attnpre(x1, wqT, wkT, wvT, asnp(L0_attn_qn_w), asnp(L0_attn_kn_w))
    x2 = run_attn(x1, qT, kT, v, woT)
    x3 = run_moe(x2, *moe_args)
    rwkv_args = (wrT, wkrT, wvrT, asnp(L1_ffn_mu_r), asnp(L1_ffn_mu_k))
    x4 = run_rwkv(x3, *rwkv_args)
    x5 = run_s6(x4, wib, wxb, wdtb, wosb, asnp(L1_attn_bi), asnp(L1_attn_bx),
                asnp(L1_attn_bdt), asnp(L1_attn_A_log), asnp(L1_attn_Dp),
                asnp(L1_attn_bo), asnp(L1_attn_gamma), asnp(L1_attn_beta))
    x6 = run_rwkv(x5, *rwkv_args)
    return np.ascontiguousarray(x6.transpose(0, 2, 1))
